# version 03
# baseline (speedup 1.0000x reference)
import math
import numpy as np
import ml_dtypes
import concourse.bass as bass
import concourse.mybir as mybir
from concourse.bass_utils import run_bass_kernel_spmd

F32 = mybir.dt.float32
BF16 = mybir.dt.bfloat16
AF = mybir.ActivationFunctionType
ALU = mybir.AluOpType
AXX = mybir.AxisListType.X

D = 2048
NH = 16
DHD = 128
SEQ = 8192
NT_SEQ = SEQ // 128
NT_OWN = 16
TCORE = NT_OWN * 128
NCORE = 8
INV_SQRT_DH = 1.0 / math.sqrt(DHD)
EPS = 1e-5
NEGBIG = -1.0e30

NE = 16
IEXP = 1408
ISH = 2816

BF = ml_dtypes.bfloat16


class Sched:
    ENGINES = ("sync", "scalar", "vector", "tensor", "gpsimd")
    TICK = {"sync": "qsp", "scalar": "act", "vector": "dve",
            "tensor": "pe", "gpsimd": "qgp"}
    STEP = {"qsp": 16, "qgp": 16, "pe": 1, "act": 1, "dve": 1}

    def __init__(self):
        self.ops = {e: [] for e in self.ENGINES}
        self.cnt = {}

    def add(self, eng, fn, deps=(), sem=None):
        sem = sem or self.TICK[eng]
        step = self.STEP[sem]
        self.cnt[sem] = self.cnt.get(sem, 0) + step
        tok = (sem, self.cnt[sem])
        deps = tuple(d for d in deps if d is not None)
        self.ops[eng].append((fn, deps, sem, step))
        return tok

    def dma(self, eng, out, in_, deps=()):
        return self.add(eng, lambda e, o=out, i=in_: e.dma_start(out=o, in_=i),
                        deps=deps)

    def trans(self, out, in_, deps=()):
        return self.add("sync",
                        lambda e, o=out, i=in_: e.dma_start_transpose(o, i),
                        deps=deps)

    def mmgrp(self, psum_ap, lhsT_slices, rhs_slices, deps=()):
        n = len(lhsT_slices)

        def fn(e, ls=tuple(lhsT_slices), rs=tuple(rhs_slices), p=psum_ap, n=n):
            inst = None
            for i in range(n):
                inst = e.matmul(p, ls[i], rs[i], start=(i == 0),
                                stop=(i == n - 1))
            return inst
        return self.add("tensor", fn, deps=deps)

    def emit(self, nc, block):
        sems = {}
        from contextlib import ExitStack
        self._es = ExitStack()
        for name in self.cnt:
            sems[name] = self._es.enter_context(nc.semaphore(name))
        hw = {}

        def mk(engname):
            def body(e):
                for fn, deps, sem, step in self.ops[engname]:
                    for dsem, dval in deps:
                        if dval > hw.get((engname, dsem), 0):
                            e.wait_ge(sems[dsem], dval)
                            hw[(engname, dsem)] = dval
                    inst = fn(e)
                    if inst is not None:
                        inst.then_inc(sems[sem], step)
            return body

        block.sync(mk("sync"))
        block.scalar(mk("scalar"))
        block.vector(mk("vector"))
        block.tensor(mk("tensor"))
        block.gpsimd(mk("gpsimd"))

        def fin(e):
            for name, v in self.cnt.items():
                e.wait_ge(sems[name], v)
        block.sync(fin)

    def close(self):
        self._es.close()


def fm3(sb, c=16):
    return sb[:].rearrange("p (c m) -> p c m", c=c)


def emit_ln(S, x_src_ap, lnfm_dst_ap, scr, deps=()):
    xt, sq, st, lt = scr["xt"], scr["sq"], scr["st"], scr["lt"]
    t_x = deps
    if x_src_ap is not None:
        t_x = (S.dma("sync", xt[:], x_src_ap, deps=deps),)
    t1 = S.add("vector", lambda e: e.tensor_reduce(
        st[:, 0:1], xt[:], AXX, ALU.add, negate=True), deps=t_x)
    t2 = S.add("scalar", lambda e: e.activation(sq[:], xt[:], AF.Square),
               deps=t_x)
    t3 = S.add("vector", lambda e: e.reduce_sum(st[:, 1:2], sq[:], axis=AXX),
               deps=(t2,))
    t4 = S.add("vector", lambda e: e.tensor_single_scalar(
        st[:, 2:3], st[:, 0:1], 1.0 / D, ALU.mult), deps=(t1,))
    t5 = S.add("vector", lambda e: e.tensor_single_scalar(
        st[:, 3:4], st[:, 1:2], 1.0 / D, ALU.mult), deps=(t3,))
    t6 = S.add("vector", lambda e: e.tensor_mul(
        st[:, 4:5], st[:, 2:3], st[:, 2:3]), deps=(t4,))
    t7 = S.add("vector", lambda e: e.tensor_sub(
        st[:, 5:6], st[:, 3:4], st[:, 4:5]), deps=(t5, t6))
    t8 = S.add("scalar", lambda e: e.activation(
        st[:, 6:7], st[:, 5:6], AF.Sqrt, bias=EPS), deps=(t7,))
    t9 = S.add("vector", lambda e: e.reciprocal(st[:, 7:8], st[:, 6:7]),
               deps=(t8,))
    ta = S.add("vector", lambda e: e.tensor_scalar(
        lt[:], xt[:], st[:, 2:3], st[:, 7:8], ALU.add, ALU.mult),
        deps=(t9,) + t_x)
    tb = S.trans(lnfm_dst_ap, lt[:], deps=(ta,))
    return tb, ta


def build_launch1():
    nc = bass.Bass("TRN2", target_bir_lowering=False)
    xseq = nc.dram_tensor("xseq", [SEQ, D], F32, kind="ExternalInput").ap()
    xown = nc.dram_tensor("xown", [TCORE, D], F32, kind="ExternalInput").ap()
    qpos = nc.dram_tensor("qpos", [1, TCORE], F32, kind="ExternalInput").ap()
    iota_in = nc.dram_tensor("iota", [128, 1], F32, kind="ExternalInput").ap()
    onesf_in = nc.dram_tensor("onesf", [1, 128], F32, kind="ExternalInput").ap()
    onesb_in = nc.dram_tensor("onesb", [128, 1], BF16, kind="ExternalInput").ap()
    ident_in = nc.dram_tensor("ident", [128, 128], F32, kind="ExternalInput").ap()
    cst_in = nc.dram_tensor("cst", [128, 8], F32, kind="ExternalInput").ap()
    cosa_in = nc.dram_tensor("cosa", [SEQ, 64], F32, kind="ExternalInput").ap()
    sina_in = nc.dram_tensor("sina", [SEQ, 64], F32, kind="ExternalInput").ap()
    wq_in = nc.dram_tensor("wqT", [D, 864], F32, kind="ExternalInput").ap()
    wkv_in = nc.dram_tensor("wkvT", [D, 576], F32, kind="ExternalInput").ap()
    wo_in = nc.dram_tensor("woT", [D, D], F32, kind="ExternalInput").ap()
    gw_in = nc.dram_tensor("gwT", [D, NE], F32, kind="ExternalInput").ap()
    w13_in = nc.dram_tensor("w13L", [NE, 22, 128, 16 * 128], F32, kind="ExternalInput").ap()
    w2_in = nc.dram_tensor("w2L", [NE, 2, 128, 11 * 1024], F32, kind="ExternalInput").ap()
    sw13_in = nc.dram_tensor("sw13L", [44, 128, 16 * 128], F32, kind="ExternalInput").ap()
    sw2_in = nc.dram_tensor("sw2L", [2, 128, 22 * 1024], F32, kind="ExternalInput").ap()
    x2o = nc.dram_tensor("x2o", [TCORE, D], F32, kind="ExternalOutput").ap()
    kfm = nc.dram_tensor("kfm", [D, SEQ], BF16).ap()
    vtok = nc.dram_tensor("vtok", [SEQ, D], BF16).ap()
    x1d = nc.dram_tensor("x1d", [TCORE, D], F32).ap()

    S = Sched()
    from contextlib import ExitStack
    with ExitStack() as TOP:
        E = TOP.enter_context
        iota = E(nc.sbuf_tensor("iota_t", [128, 1], F32))
        onesf = E(nc.sbuf_tensor("onesf_t", [1, 128], F32))
        onesb = E(nc.sbuf_tensor("onesb_t", [128, 1], BF16))
        ident = E(nc.sbuf_tensor("ident_t", [128, 128], F32))
        cstt = E(nc.sbuf_tensor("cst_t", [128, 8], F32))
        qpos_t = E(nc.sbuf_tensor("qpos_t", [1, TCORE], F32))
        nc.const_aps.register(F32, EPS, cstt[:, 0:1])
        nc.const_aps.register(F32, INV_SQRT_DH, cstt[:, 1:2])
        G = dict(iota=iota, onesf=onesf, onesb=onesb, ident=ident, qpos=qpos_t)

        t_const = tuple(
            S.dma("sync", dst[:], src)
            for dst, src in ((iota, iota_in), (onesf, onesf_in),
                             (onesb, onesb_in), (ident, ident_in),
                             (cstt, cst_in), (qpos_t, qpos)))

        t_attn = emit_attn_a(S, nc, E, t_const, G, xseq, xown, x1d, kfm,
                             vtok, cosa_in, sina_in, wq_in, wkv_in, wo_in)
        emit_ffn(S, nc, E, t_const, G, x_in=x1d, x_out=x2o, gw=gw_in,
                 w13=w13_in, w2=w2_in, sw13=sw13_in, sw2=sw2_in,
                 t_start=t_attn[-4:])

        with nc.Block() as block:
            S.emit(nc, block)
        S.close()
    return nc


def emit_attn_a(S, nc, E, t_const, G, xseq, xown, x_next, kfm, vtok,
                cosK, sinK, wq, wkv, wo):
    from contextlib import ExitStack
    with ExitStack() as PH:
        A = PH.enter_context
        q_fm = A(nc.sbuf_tensor("a_qfm", [128, NH * TCORE], BF16))
        comb = q_fm
        qfm3 = q_fm[:].rearrange("p (h m) -> p h m", h=16)
        kfm3 = kfm.rearrange("(c p) m -> p c m", c=16)

        with ExitStack() as S1:
            B = S1.enter_context
            wkv_bf = B(nc.sbuf_tensor("wkv_bf", [128, 16 * 576], BF16))
            wq_bf = B(nc.sbuf_tensor("wq_bf", [128, 16 * 864], BF16))
            t_wkv = S.dma("gpsimd", fm3(wkv_bf),
                          wkv.rearrange("(c p) m -> p c m", c=16))
            t_wq = S.dma("gpsimd", fm3(wq_bf),
                         wq.rearrange("(c p) m -> p c m", c=16))
            NB = 2
            ln_scr = [dict(xt=B(nc.sbuf_tensor(f"a_xt{i}", [128, D], F32)),
                           sq=B(nc.sbuf_tensor(f"a_sq{i}", [128, D], F32)),
                           st=B(nc.sbuf_tensor(f"a_st{i}", [128, 8], F32)),
                           lt=B(nc.sbuf_tensor(f"a_lt{i}", [128, D], BF16)))
                      for i in range(NB)]
            lnfm = [B(nc.sbuf_tensor(f"a_lnfm{i}", [128, 16 * 128], BF16))
                    for i in range(NB)]
            ktok = [B(nc.sbuf_tensor(f"a_ktok{i}", [128, D], BF16)) for i in range(NB)]
            vtokt = [B(nc.sbuf_tensor(f"a_vtok{i}", [128, D], BF16)) for i in range(NB)]
            ktr = B(nc.sbuf_tensor("a_ktr", [128, 16 * 128], BF16))
            cost = [B(nc.sbuf_tensor(f"a_cos{i}", [128, 64], F32)) for i in range(NB)]
            sint = [B(nc.sbuf_tensor(f"a_sin{i}", [128, 64], F32)) for i in range(NB)]
            bkv = [B(nc.sbuf_tensor(f"a_bkv{i}", [128, 512], F32)) for i in range(NB)]
            akv = [B(nc.sbuf_tensor(f"a_akv{i}", [128, 64], F32)) for i in range(NB)]
            rtmp = [B(nc.sbuf_tensor(f"a_rt{i}", [128, 256], F32)) for i in range(NB)]
            p_b = [B(nc.psum_tensor(f"a_pb{i}", [128, 512], F32)) for i in range(NB)]
            p_a = [B(nc.psum_tensor(f"a_pa{i}", [128, 64], F32)) for i in range(NB)]
            bq = B(nc.sbuf_tensor("a_bq", [128, 768], F32))
            aq = B(nc.sbuf_tensor("a_aq", [128, 96], F32))
            qtok = B(nc.sbuf_tensor("a_qtok", [128, D], BF16))
            qacc = B(nc.sbuf_tensor("a_qacc", [128, 128], F32))
            p_q0 = B(nc.psum_tensor("a_pq0", [128, 512], F32))
            p_q1 = B(nc.psum_tensor("a_pq1", [128, 352], F32))

            prev = {}
            kv_toks = []
            for g in range(NT_SEQ):
                i = g % NB
                war = prev.get(("kv", i), ())
                t_tr, _ = emit_ln(S, xseq[g * 128:(g + 1) * 128, :],
                                  fm3(lnfm[i], 16), ln_scr[i],
                                  deps=t_const + war)
                lh = [lnfm[i][:, k * 128:(k + 1) * 128] for k in range(16)]
                tb = S.mmgrp(p_b[i][:], lh,
                             [wkv_bf[:, k * 576:k * 576 + 512]
                              for k in range(16)],
                             deps=(t_tr, t_wkv) + war)
                ta = S.mmgrp(p_a[i][:], lh,
                             [wkv_bf[:, k * 576 + 512:(k + 1) * 576]
                              for k in range(16)],
                             deps=(t_tr, t_wkv) + war)
                tasc = S.add("vector", lambda e, i=i: e.tensor_single_scalar(
                    akv[i][:], p_a[i][:], 0.5, ALU.mult), deps=(ta,) + war)
                tc1 = S.dma("sync", cost[i][:],
                            cosK[g * 128:(g + 1) * 128, :], deps=war)
                tc2 = S.dma("sync", sint[i][:],
                            sinK[g * 128:(g + 1) * 128, :], deps=war)
                trs = []
                for r in range(4):
                    c0 = r * 128
                    x1a = p_b[i][:, c0:c0 + 64]
                    x2a = p_b[i][:, c0 + 64:c0 + 128]
                    tm1 = S.add("vector", lambda e, x1a=x1a, i=i: e.tensor_mul(
                        rtmp[i][:, 0:64], x1a, cost[i][:]), deps=(tb, tc1))
                    tm2 = S.add("vector", lambda e, x2a=x2a, i=i: e.tensor_mul(
                        rtmp[i][:, 64:128], x2a, sint[i][:]), deps=(tb, tc2))
                    ty1 = S.add("vector", lambda e, i=i, c0=c0: e.tensor_add(
                        bkv[i][:, c0:c0 + 64], rtmp[i][:, 0:64],
                        rtmp[i][:, 64:128]), deps=(tm1, tm2))
                    tm3 = S.add("vector", lambda e, x1a=x1a, i=i: e.tensor_mul(
                        rtmp[i][:, 128:192], x1a, sint[i][:]), deps=(tb, tc2))
                    tm4 = S.add("vector", lambda e, x2a=x2a, i=i: e.tensor_mul(
                        rtmp[i][:, 192:256], x2a, cost[i][:]), deps=(tb, tc1))
                    ty2 = S.add("vector", lambda e, i=i, c0=c0: e.tensor_sub(
                        bkv[i][:, c0 + 64:c0 + 128], rtmp[i][:, 192:256],
                        rtmp[i][:, 128:192]), deps=(tm3, tm4))
                    trs += [ty1, ty2]
                trs = tuple(trs)
                tks = []
                for h in range(NH):
                    a0 = akv[i][:, 0 * 16 + h:0 * 16 + h + 1]
                    a1 = akv[i][:, 1 * 16 + h:1 * 16 + h + 1]
                    a2 = akv[i][:, 2 * 16 + h:2 * 16 + h + 1]
                    a3 = akv[i][:, 3 * 16 + h:3 * 16 + h + 1]
                    tt0 = S.add("vector", lambda e, i=i, a0=a0: e.tensor_scalar(
                        rtmp[i][:, 0:128], bkv[i][:, 0:128], a0, None,
                        ALU.mult), deps=trs + (tasc,))
                    tt1 = S.add("vector", lambda e, i=i, a1=a1, h=h:
                                e.scalar_tensor_tensor(
                                    ktok[i][:, h * 128:(h + 1) * 128],
                                    bkv[i][:, 128:256], a1, rtmp[i][:, 0:128],
                                    ALU.mult, ALU.add), deps=(tt0,))
                    tt2 = S.add("vector", lambda e, i=i, a2=a2: e.tensor_scalar(
                        rtmp[i][:, 128:256], bkv[i][:, 256:384], a2, None,
                        ALU.mult), deps=trs + (tasc,))
                    tt3 = S.add("vector", lambda e, i=i, a3=a3, h=h:
                                e.scalar_tensor_tensor(
                                    vtokt[i][:, h * 128:(h + 1) * 128],
                                    bkv[i][:, 384:512], a3, rtmp[i][:, 128:256],
                                    ALU.mult, ALU.add), deps=(tt2,))
                    tks += [tt1, tt3]
                tks = tuple(tks)
                tvs = S.dma("sync", vtok[g * 128:(g + 1) * 128, :],
                            vtokt[i][:], deps=tks)
                ttr = S.trans(fm3(ktr, 16), ktok[i][:],
                              deps=tks + prev.get("ktr", ()))
                tks2 = S.dma("sync", kfm3[:, :, g * 128:(g + 1) * 128],
                             fm3(ktr, 16), deps=(ttr,))
                prev["ktr"] = (tks2,)
                prev[("kv", i)] = (tvs, tks2, tasc)
                kv_toks += [tvs, tks2]

            q_toks = []
            for s in range(NT_OWN):
                i = s % NB
                war = prev.get(("kv", i), ()) + prev.get("q", ())
                t_tr, _ = emit_ln(S, xown[s * 128:(s + 1) * 128, :],
                                  fm3(lnfm[i], 16), ln_scr[i],
                                  deps=t_const + war)
                lh = [lnfm[i][:, k * 128:(k + 1) * 128] for k in range(16)]
                t0 = S.mmgrp(p_q0[:], lh,
                             [wq_bf[:, k * 864:k * 864 + 512]
                              for k in range(16)],
                             deps=(t_tr, t_wq) + war)
                t1 = S.mmgrp(p_q1[:], lh,
                             [wq_bf[:, k * 864 + 512:(k + 1) * 864]
                              for k in range(16)],
                             deps=(t_tr, t_wq) + war)
                tb0 = S.add("vector", lambda e: e.tensor_copy(
                    bq[:, 0:512], p_q0[:]), deps=(t0,) + prev.get("q", ()))
                tb1 = S.add("vector", lambda e: e.tensor_copy(
                    bq[:, 512:768], p_q1[:, 0:256]), deps=(t1,))
                tav = S.add("vector", lambda e: e.tensor_single_scalar(
                    aq[:], p_q1[:, 256:352], 1.0 / 6.0, ALU.mult),
                    deps=(t1,))
                qs = []
                for h in range(NH):
                    acc = None
                    for r in range(6):
                        acol = aq[:, r * 16 + h:r * 16 + h + 1]
                        bsl = bq[:, r * 128:(r + 1) * 128]
                        if r == 0:
                            acc = S.add("vector", lambda e, acol=acol,
                                        bsl=bsl: e.tensor_scalar(
                                            qacc[:], bsl, acol, None,
                                            ALU.mult), deps=(tb0, tb1, tav))
                        elif r < 5:
                            acc = S.add("vector", lambda e, acol=acol,
                                        bsl=bsl: e.scalar_tensor_tensor(
                                            qacc[:], bsl, acol, qacc[:],
                                            ALU.mult, ALU.add), deps=(acc,))
                        else:
                            acc = S.add("vector", lambda e, acol=acol,
                                        bsl=bsl, h=h: e.scalar_tensor_tensor(
                                            qtok[:, h * 128:(h + 1) * 128],
                                            bsl, acol, qacc[:],
                                            ALU.mult, ALU.add), deps=(acc,))
                    qs.append(acc)
                ttr = S.trans(qfm3[:, :, s * 128:(s + 1) * 128], qtok[:],
                              deps=tuple(qs))
                prev["q"] = (ttr,)
                q_toks.append(ttr)
            deps_ready = tuple(kv_toks[-4:]) + tuple(q_toks)

        t_comb = []
        with ExitStack() as S2:
            B = S2.enter_context
            vh = [B(nc.sbuf_tensor(f"a_vh{i}", [128, 64 * 128], BF16))
                  for i in range(2)]
            kt = [B(nc.sbuf_tensor(f"a_kt{i}", [128, 8 * 128], BF16))
                  for i in range(2)]
            pt = [B(nc.sbuf_tensor(f"a_pt{i}", [128, 512], BF16))
                  for i in range(2)]
            m01 = B(nc.sbuf_tensor("a_m01", [128, 16 * 512], BF16))
            dsb = B(nc.sbuf_tensor("a_dsb", [128, 512], F32))
            rden = B(nc.sbuf_tensor("a_rden", [1, 512], F32))
            rdsb = B(nc.sbuf_tensor("a_rdsb", [128, 512], F32))
            p_s = [B(nc.psum_tensor(f"a_ps{i}", [128, 512], F32))
                   for i in range(2)]
            p_oo = [B(nc.psum_tensor(f"a_po{i}", [128, 512], F32))
                    for i in range(2)]
            p_css = [B(nc.psum_tensor(f"a_pcs{i}", [1, 512], F32))
                     for i in range(2)]
            p_bc = B(nc.psum_tensor("a_pbc", [128, 512], F32))
            onesf, onesb, iota, qpos_t = (G["onesf"], G["onesb"], G["iota"],
                                          G["qpos"])
            vtok3 = vtok.rearrange("(g p) m -> p g m", g=64)
            last = {}
            for p in range(4):
                BP = 16 * (p + 1)
                tqb = S.add("tensor", lambda e, p=p: e.matmul(
                    p_bc[:], onesf[:], qpos_t[0:1, p * 512:(p + 1) * 512],
                    start=True, stop=True),
                    deps=t_const + last.get("bc", ()) + deps_ready)
                tD = S.add("vector", lambda e: e.tensor_scalar(
                    dsb[:], p_bc[:], iota[:, 0:1], None, ALU.subtract),
                    deps=(tqb,) + last.get("dsb", ()))
                last["bc"] = (tD,)
                tms = []
                for t in range(16):
                    tm = S.add("vector", lambda e, t=t, p=p:
                               e.tensor_single_scalar(
                                   m01[:, t * 512:(t + 1) * 512], dsb[:],
                                   float(128 * (16 * p + t)), ALU.is_lt),
                               deps=(tD,) + last.get("m01", ()))
                    tms.append(tm)
                last["dsb"] = (tms[-1],)
                m01_toks = (tms[-1],)
                quad_done = []
                for h in range(NH):
                    iv = h % 2
                    io = h % 2
                    p_o = p_oo[io]
                    p_cs = p_css[io]
                    tvl = S.dma("sync", fm3(vh[iv], 64),
                                vtok3[:, :, h * 128:(h + 1) * 128],
                                deps=deps_ready + last.get(("v", iv), ()))
                    t_cs_last = None
                    t_pv_last = None
                    tkls = {}
                    for t in range(BP):
                        ip = t % 2
                        if t % 8 == 0:
                            ik = (t // 8) % 2
                            tkls[ik] = S.dma(
                                "sync", kt[ik][:],
                                kfm[h * 128:(h + 1) * 128,
                                    t * 128:(t + 8) * 128],
                                deps=deps_ready + last.get(("k", ik), ()))
                        ik = (t // 8) % 2
                        tkl = tkls[ik]
                        ksl = kt[ik][:, (t % 8) * 128:(t % 8 + 1) * 128]
                        tmm = S.add("tensor", lambda e, ksl=ksl, ip=ip, h=h,
                                    p=p: e.matmul(
                                        p_s[ip][:], ksl,
                                        q_fm[:, h * TCORE + p * 512:
                                             h * TCORE + p * 512 + 512],
                                        start=True, stop=True),
                                    deps=(tkl,) + last.get(("s", ip), ()))
                        if t % 8 == 7 or t == BP - 1:
                            last[("k", ik)] = (tmm,)
                        tmm2 = tmm
                        if t >= 16 * p:
                            td = t - 16 * p
                            tmm2 = S.add("vector", lambda e, ip=ip, td=td:
                                         e.scalar_tensor_tensor(
                                             p_s[ip][:],
                                             m01[:, td * 512:(td + 1) * 512],
                                             NEGBIG, p_s[ip][:], ALU.mult,
                                             ALU.add),
                                         deps=(tmm,) + m01_toks)
                        texp = S.add("scalar", lambda e, ip=ip: e.activation(
                            pt[ip][:], p_s[ip][:], AF.Exp,
                            scale=INV_SQRT_DH),
                            deps=(tmm2,) + last.get(("pt", ip), ()))
                        last[("s", ip)] = (texp,)
                        tcs = S.add("tensor",
                                    lambda e, ip=ip, t=t, BP=BP, p_cs=p_cs:
                                    e.matmul(p_cs[:], onesb[:], pt[ip][:],
                                             start=(t == 0),
                                             stop=(t == BP - 1)),
                                    deps=(texp,) + t_const +
                                    (last.get(("pcs", io), ()) if t == 0 else ()))
                        tpv = S.add("tensor",
                                    lambda e, ip=ip, t=t, iv=iv, BP=BP, p_o=p_o:
                                    e.matmul(p_o[:],
                                             vh[iv][:, t * 128:(t + 1) * 128],
                                             pt[ip][:], start=(t == 0),
                                             stop=(t == BP - 1)),
                                    deps=(texp, tvl) +
                                    (last.get(("po", io), ()) if t == 0 else ()))
                        last[("pt", ip)] = (tcs, tpv)
                        t_cs_last, t_pv_last = tcs, tpv
                    last[("v", iv)] = (t_pv_last,)
                    trd = S.add("vector", lambda e, p_cs=p_cs: e.reciprocal(
                        rden[:], p_cs[:]),
                        deps=(t_cs_last,) + last.get("rden", ()))
                    last[("pcs", io)] = (trd,)
                    tbc = S.add("tensor", lambda e: e.matmul(
                        p_bc[:], onesf[:], rden[:], start=True, stop=True),
                        deps=(trd,) + last.get("bc", ()))
                    last["rden"] = (tbc,)
                    tcp = S.add("vector", lambda e: e.tensor_copy(
                        rdsb[:], p_bc[:]), deps=(tbc,) + last.get("rdsb", ()))
                    last["bc"] = (tcp,)
                    tcb = S.add("vector", lambda e, h=h, p=p, p_o=p_o:
                                e.tensor_mul(
                                    comb[:, h * 2048 + p * 512:
                                         h * 2048 + p * 512 + 512],
                                    p_o[:], rdsb[:]), deps=(tcp, t_pv_last))
                    last["rdsb"] = (tcb,)
                    last[("po", io)] = (tcb,)
                    quad_done.append(tcb)
                    t_comb.append(tcb)
                last["m01"] = (quad_done[0],) if quad_done else ()
                last["m01"] = tuple(quad_done)

        t_all = tuple(t_comb)
        with ExitStack() as S3:
            B = S3.enter_context
            wo_bf = B(nc.sbuf_tensor("a_wobf", [128, 16 * 2048], BF16))
            t_wo = S.dma("gpsimd", fm3(wo_bf),
                         wo.rearrange("(c p) m -> p c m", c=16), deps=t_all)
            xo = [B(nc.sbuf_tensor(f"a_xo{i}", [128, 512], F32))
                  for i in range(2)]
            x1t = [B(nc.sbuf_tensor(f"a_x1t{i}", [128, 512], F32))
                   for i in range(2)]
            p_y = [B(nc.psum_tensor(f"a_py{i}", [128, 512], F32))
                   for i in range(2)]
            pr = {}
            t_res = []
            for s in range(NT_OWN):
                for oc in range(4):
                    i = (s * 4 + oc) % 2
                    war = pr.get(i, ()) + (t_all[-1],)
                    tmm = S.mmgrp(
                        p_y[i][:],
                        [comb[:, c * 2048 + s * 128:c * 2048 + (s + 1) * 128]
                         for c in range(16)],
                        [wo_bf[:, c * 2048 + oc * 512:c * 2048 + oc * 512 + 512]
                         for c in range(16)],
                        deps=t_all + (t_wo,) + war)
                    txl = S.dma("sync", xo[i][:],
                                xown[s * 128:(s + 1) * 128,
                                     oc * 512:(oc + 1) * 512], deps=war)
                    tad = S.add("vector", lambda e, i=i: e.tensor_add(
                        x1t[i][:], p_y[i][:], xo[i][:]), deps=(tmm, txl))
                    tst = S.dma("sync",
                                x_next[s * 128:(s + 1) * 128,
                                       oc * 512:(oc + 1) * 512],
                                x1t[i][:], deps=(tad,))
                    pr[i] = (tst, tad)
                    t_res.append(tst)
            return tuple(t_res)


def emit_ffn(S, nc, E, t_const, G, x_in, x_out, gw, w13, w2, sw13, sw2,
             t_start=()):
    onesf, ident = G["onesf"], G["ident"]
    from contextlib import ExitStack
    with ExitStack() as PH:
        A = PH.enter_context
        lnfm = A(nc.sbuf_tensor("f_lnfm", [128, 16 * 1024], BF16))
        cmb = A(nc.sbuf_tensor("f_cmb", [128, 8 * 16], F32))
        h_sb = A(nc.sbuf_tensor("f_h", [128, 11 * 1024], BF16))
        w2bf = A(nc.sbuf_tensor("f_w2", [128, 11 * 1024], BF16))
        acc = A(nc.sbuf_tensor("f_acc", [128, 8 * 2048], F32))
        wg = [A(nc.sbuf_tensor(f"f_wg{i}", [128, 16 * 128], BF16)) for i in range(2)]
        wu = [A(nc.sbuf_tensor(f"f_wu{i}", [128, 16 * 128], BF16)) for i in range(2)]
        gwbf = A(nc.sbuf_tensor("f_gw", [128, 16 * 16], BF16))
        sg = [A(nc.sbuf_tensor(f"f_sg{i}", [128, 512], F32)) for i in range(2)]
        h32 = [A(nc.sbuf_tensor(f"f_h32{i}", [128, 512], F32)) for i in range(2)]
        rt = A(nc.sbuf_tensor("f_rt", [128, 16], F32))
        rstat = A(nc.sbuf_tensor("f_rs", [128, 40], F32))
        ln_scr = dict(xt=A(nc.sbuf_tensor("f_xt", [128, D], F32)),
                      sq=A(nc.sbuf_tensor("f_sq", [128, D], F32)),
                      st=A(nc.sbuf_tensor("f_st", [128, 8], F32)),
                      lt=A(nc.sbuf_tensor("f_lt", [128, D], BF16)))
        xr = [A(nc.sbuf_tensor(f"f_xr{i}", [128, 512], F32)) for i in range(2)]
        p_g = [A(nc.psum_tensor(f"fp_g{i}", [128, 512], F32)) for i in range(2)]
        p_u = [A(nc.psum_tensor(f"fp_u{i}", [128, 512], F32)) for i in range(2)]
        p_r = A(nc.psum_tensor("fp_r", [128, 16], F32))
        p_y = [A(nc.psum_tensor(f"fp_y{i}", [128, 512], F32)) for i in range(1)]

        t_gw = S.dma("gpsimd", fm3(gwbf), gw.rearrange("(c p) m -> p c m", c=16),
                     deps=t_start)
        prev = {}
        lnfm3 = fm3(lnfm)

        for blk in range(2):
            ln_toks = []
            rt_toks = []
            for st8 in range(8):
                s = blk * 8 + st8
                war = prev.get("ln", ()) + prev.get("xr", ()) + \
                    prev.get("lnread", ()) + (t_start if (blk == 0 and st8 < 2) else ())
                dst = lnfm3[:, :, st8 * 128:(st8 + 1) * 128]
                t_tr, _ = emit_ln(S, x_in[s * 128:(s + 1) * 128, :], dst,
                                  ln_scr, deps=t_const + war)
                prev["ln"] = (t_tr,)
                ln_toks.append(t_tr)
                war_r = prev.get("rt", ())
                tr = S.mmgrp(
                    p_r[:],
                    [lnfm[:, k * 1024 + st8 * 128:k * 1024 + (st8 + 1) * 128]
                     for k in range(16)],
                    [gwbf[:, k * 16:(k + 1) * 16] for k in range(16)],
                    deps=(t_tr, t_gw) + war_r)
                c0 = S.add("vector", lambda e: e.reduce_max(
                    rstat[:, 0:1], p_r[:], axis=AXX),
                    deps=(tr,) + prev.get("rs", ()))
                c1 = S.add("vector", lambda e: e.tensor_single_scalar(
                    rstat[:, 1:2], rstat[:, 0:1], -1.0, ALU.mult), deps=(c0,))
                c2 = S.add("scalar", lambda e: e.activation(
                    rt[:], p_r[:], AF.Exp, bias=rstat[:, 1:2]),
                    deps=(c1, tr) + prev.get("rtb", ()))
                c3 = S.add("vector", lambda e: e.reduce_sum(
                    rstat[:, 2:3], rt[:], axis=AXX), deps=(c2,))
                c4 = S.add("vector", lambda e: e.reciprocal(
                    rstat[:, 3:4], rstat[:, 2:3]), deps=(c3,))
                c5 = S.add("vector", lambda e: e.tensor_scalar(
                    rt[:], rt[:], rstat[:, 3:4], None, ALU.mult), deps=(c4,))
                c6 = S.add("vector", lambda e: e.max(rstat[:, 8:16], rt[:]),
                           deps=(c5,))
                c7 = S.add("vector", lambda e: e.tensor_scalar(
                    rstat[:, 16:32], rt[:], rstat[:, 13:14], None, ALU.is_ge),
                    deps=(c6,))
                c8 = S.add("vector", lambda e: e.tensor_mul(
                    rstat[:, 16:32], rstat[:, 16:32], rt[:]), deps=(c7,))
                c9 = S.add("vector", lambda e: e.reduce_sum(
                    rstat[:, 4:5], rstat[:, 16:32], axis=AXX), deps=(c8,))
                ca = S.add("vector", lambda e: e.tensor_single_scalar(
                    rstat[:, 5:6], rstat[:, 4:5], 1e-20, ALU.add), deps=(c9,))
                cb = S.add("vector", lambda e: e.reciprocal(
                    rstat[:, 6:7], rstat[:, 5:6]), deps=(ca,))
                ce = S.add("vector", lambda e, st8=st8: e.tensor_scalar(
                    cmb[:, st8 * 16:(st8 + 1) * 16], rstat[:, 16:32],
                    rstat[:, 6:7], None, ALU.mult),
                    deps=(cb,) + prev.get("ctw", ()))
                prev["rt"] = (c2,)
                prev["rtb"] = (c8,)
                prev["rs"] = (ce,)
                rt_toks.append(ce)
            t_ln = tuple(ln_toks)
            t_rt = tuple(rt_toks)

            blk_done = []
            for e_i in range(18):
                is_sh = e_i >= 16
                sh_h = e_i - 16
                h_toks = []
                for ii in range(11):
                    iw = ii % 2
                    war_w = prev.get(("w13", iw), ()) + \
                        (t_start if e_i == 0 and ii < 2 and blk == 0 else ())
                    if not is_sh:
                        gsrc = w13[e_i, ii]
                        usrc = w13[e_i, 11 + ii]
                    else:
                        gsrc = sw13[sh_h * 11 + ii]
                        usrc = sw13[22 + sh_h * 11 + ii]
                    t_wg = S.dma("gpsimd", wg[iw][:], gsrc, deps=war_w)
                    t_wu = S.dma("gpsimd", wu[iw][:], usrc, deps=war_w)
                    for tc in range(2):
                        ip = tc
                        war_p = prev.get(("pg", ip), ())
                        tg = S.mmgrp(
                            p_g[ip][:],
                            [wg[iw][:, k * 128:(k + 1) * 128] for k in range(16)],
                            [lnfm[:, k * 1024 + tc * 512:
                                  k * 1024 + tc * 512 + 512] for k in range(16)],
                            deps=t_ln + (t_wg,) + war_p)
                        tu = S.mmgrp(
                            p_u[ip][:],
                            [wu[iw][:, k * 128:(k + 1) * 128] for k in range(16)],
                            [lnfm[:, k * 1024 + tc * 512:
                                  k * 1024 + tc * 512 + 512] for k in range(16)],
                            deps=t_ln + (t_wu,) + war_p)
                        ts1 = S.add("scalar", lambda e, ip=ip: e.activation(
                            sg[ip][:], p_g[ip][:], AF.Silu),
                            deps=(tg,) + prev.get(("sg", ip), ()))
                        ts2 = S.add("vector", lambda e, ip=ip: e.tensor_mul(
                            h32[ip][:], sg[ip][:], p_u[ip][:]), deps=(ts1, tu))
                        th = S.add("vector", lambda e, ip=ip, ii=ii, tc=tc:
                                   e.tensor_copy(
                                       h_sb[:, ii * 1024 + tc * 512:
                                            ii * 1024 + tc * 512 + 512],
                                       h32[ip][:]), deps=(ts2,))
                        prev[("pg", ip)] = (th,)
                        prev[("sg", ip)] = (ts2,)
                        h_toks.append(th)
                    prev[("w13", iw)] = (tg, tu)
                t_h = tuple(h_toks)
                y_toks = []
                for half in range(2):
                    war_w2 = prev.get("w2", ()) + \
                        (t_start if e_i == 0 and half == 0 and blk == 0 else ())
                    if not is_sh:
                        w2src = w2[e_i, half]
                    else:
                        w2src = sw2[half][:, sh_h * 11 * 1024:
                                          (sh_h + 1) * 11 * 1024]
                    t_w2 = S.dma("gpsimd", w2bf[:], w2src, deps=war_w2)
                    for tc2 in range(8):
                        for oc2 in range(2):
                            oc = half * 2 + oc2
                            tm = S.mmgrp(
                                p_y[0][:],
                                [h_sb[:, ii * 1024 + tc2 * 128:
                                      ii * 1024 + (tc2 + 1) * 128]
                                 for ii in range(11)],
                                [w2bf[:, ii * 1024 + oc2 * 512:
                                      ii * 1024 + oc2 * 512 + 512]
                                 for ii in range(11)],
                                deps=t_h + (t_w2,) + prev.get("py", ()))
                            asl = acc[:, tc2 * 2048 + oc * 512:
                                      tc2 * 2048 + oc * 512 + 512]
                            ccol = cmb[:, tc2 * 16 + e_i:tc2 * 16 + e_i + 1] \
                                if not is_sh else None
                            if e_i == 0:
                                ta = S.add("vector", lambda e, asl=asl,
                                           ccol=ccol: e.tensor_scalar(
                                               asl, p_y[0][:], ccol, None,
                                               ALU.mult),
                                           deps=(tm,) + t_rt +
                                           prev.get(("acc", tc2, oc), ()))
                            elif not is_sh:
                                ta = S.add("vector", lambda e, asl=asl,
                                           ccol=ccol: e.scalar_tensor_tensor(
                                               asl, p_y[0][:], ccol, asl,
                                               ALU.mult, ALU.add),
                                           deps=(tm,) + t_rt +
                                           prev.get(("acc", tc2, oc), ()))
                            else:
                                ta = S.add("vector", lambda e, asl=asl:
                                           e.tensor_add(asl, asl, p_y[0][:]),
                                           deps=(tm,) + prev.get(("acc", tc2, oc), ()))
                            prev["py"] = (ta,)
                            prev[("acc", tc2, oc)] = (ta,)
                            y_toks.append(ta)
                    prev["w2"] = tuple(y_toks[-16:])
                blk_done += y_toks
            prev["lnread"] = tuple(blk_done[-4:])
            prev["ctw"] = tuple(blk_done[-4:])

            for tc2 in range(8):
                s = blk * 8 + tc2
                sts = []
                for oc in range(4):
                    i = oc % 2
                    war = prev.get(("xrb", i), ())
                    txl = S.dma("sync", xr[i][:],
                                x_in[s * 128:(s + 1) * 128,
                                     oc * 512:(oc + 1) * 512], deps=war)
                    asl = acc[:, tc2 * 2048 + oc * 512:tc2 * 2048 + oc * 512 + 512]
                    osl = ln_scr["sq"][:, oc * 512:(oc + 1) * 512]
                    tad = S.add("vector", lambda e, asl=asl, osl=osl, i=i:
                                e.tensor_add(osl, asl, xr[i][:]),
                                deps=(txl,) + prev.get(("acc", tc2, oc), ()) +
                                prev.get(("xrb", i), ()) + prev.get("ln", ()))
                    tst = S.dma("sync",
                                x_out[s * 128:(s + 1) * 128,
                                      oc * 512:(oc + 1) * 512], osl,
                                deps=(tad,))
                    prev[("acc", tc2, oc)] = (tad,)
                    prev[("xrb", i)] = (tst, tad)
                    sts.append(tst)
                prev["xr"] = tuple(sts)


def build_launch2():
    nc = bass.Bass("TRN2", target_bir_lowering=False)
    xseq = nc.dram_tensor("xseq", [SEQ, D], F32, kind="ExternalInput").ap()
    xown = nc.dram_tensor("xown", [TCORE, D], F32, kind="ExternalInput").ap()
    iota_in = nc.dram_tensor("iota", [128, 1], F32, kind="ExternalInput").ap()
    onesf_in = nc.dram_tensor("onesf", [1, 128], F32, kind="ExternalInput").ap()
    onesb_in = nc.dram_tensor("onesb", [128, 1], BF16, kind="ExternalInput").ap()
    ident_in = nc.dram_tensor("ident", [128, 128], F32, kind="ExternalInput").ap()
    cst_in = nc.dram_tensor("cst", [128, 8], F32, kind="ExternalInput").ap()
    cosk_in = nc.dram_tensor("coskf", [D // 2, SEQ], F32, kind="ExternalInput").ap()
    sink_in = nc.dram_tensor("sinkf", [D // 2, SEQ], F32, kind="ExternalInput").ap()
    cosq_in = nc.dram_tensor("cosqf", [D // 2, TCORE], F32, kind="ExternalInput").ap()
    sinq_in = nc.dram_tensor("sinqf", [D // 2, TCORE], F32, kind="ExternalInput").ap()
    gate_in = nc.dram_tensor("gater", [1, NH], F32, kind="ExternalInput").ap()
    wqb_in = nc.dram_tensor("wqbT", [D, D], F32, kind="ExternalInput").ap()
    wkb_in = nc.dram_tensor("wkbT", [D, D], F32, kind="ExternalInput").ap()
    wvb_in = nc.dram_tensor("wvbT", [D, D], F32, kind="ExternalInput").ap()
    wob_in = nc.dram_tensor("wobT", [D, D], F32, kind="ExternalInput").ap()
    gw_in = nc.dram_tensor("gwT", [D, NE], F32, kind="ExternalInput").ap()
    w13_in = nc.dram_tensor("w13L", [NE, 22, 128, 16 * 128], F32, kind="ExternalInput").ap()
    w2_in = nc.dram_tensor("w2L", [NE, 2, 128, 11 * 1024], F32, kind="ExternalInput").ap()
    sw13_in = nc.dram_tensor("sw13L", [44, 128, 16 * 128], F32, kind="ExternalInput").ap()
    sw2_in = nc.dram_tensor("sw2L", [2, 128, 22 * 1024], F32, kind="ExternalInput").ap()
    yout = nc.dram_tensor("y", [TCORE, D], F32, kind="ExternalOutput").ap()
    lnseq = nc.dram_tensor("lnseq", [D, SEQ], BF16).ap()
    lnown = nc.dram_tensor("lnown", [D, TCORE], BF16).ap()
    kfm = nc.dram_tensor("kfm", [D, SEQ], BF16).ap()
    kftok = nc.dram_tensor("kftok", [SEQ, D], BF16).ap()
    vtok = nc.dram_tensor("vtok", [SEQ, D], BF16).ap()
    qfm_d = nc.dram_tensor("qfm_d", [D, TCORE], BF16).ap()
    qff_d = nc.dram_tensor("qff_d", [D, TCORE], BF16).ap()
    comb_d = nc.dram_tensor("comb_d", [D, TCORE], BF16).ap()
    x3d = nc.dram_tensor("x3d", [TCORE, D], F32).ap()

    S = Sched()
    from contextlib import ExitStack
    with ExitStack() as TOP:
        E = TOP.enter_context
        iota = E(nc.sbuf_tensor("iota_t", [128, 1], F32))
        onesf = E(nc.sbuf_tensor("onesf_t", [1, 128], F32))
        onesb = E(nc.sbuf_tensor("onesb_t", [128, 1], BF16))
        ident = E(nc.sbuf_tensor("ident_t", [128, 128], F32))
        cstt = E(nc.sbuf_tensor("cst_t", [128, 8], F32))
        gater = E(nc.sbuf_tensor("gater_t", [1, 16], F32))
        gs = E(nc.sbuf_tensor("gs_t", [1, 16], F32))
        gm = E(nc.sbuf_tensor("gm_t", [1, 16], F32))
        nc.const_aps.register(F32, EPS, cstt[:, 0:1])
        nc.const_aps.register(F32, INV_SQRT_DH, cstt[:, 1:2])
        G = dict(iota=iota, onesf=onesf, onesb=onesb, ident=ident)

        t_const = tuple(
            S.dma("sync", dst[:], src)
            for dst, src in ((iota, iota_in), (onesf, onesf_in),
                             (onesb, onesb_in), (ident, ident_in),
                             (cstt, cst_in), (gater, gate_in)))
        tg1 = S.add("scalar", lambda e: e.activation(
            gs[:], gater[:], AF.Sigmoid), deps=t_const)
        tg2 = S.add("vector", lambda e: e.tensor_single_scalar(
            gm[:], gs[:], -1.0, ALU.mult), deps=(tg1,))
        tg3 = S.add("vector", lambda e: e.tensor_single_scalar(
            gm[:], gm[:], 1.0, ALU.add), deps=(tg2,))
        t_gate = (tg1, tg3)

        t_attn = emit_attn_b(
            S, nc, E, t_const, G, xseq, xown, x3d,
            lnseq, lnown, kfm, kftok, vtok, qfm_d, qff_d, comb_d,
            cosk_in, sink_in, cosq_in, sinq_in, gs, gm, t_gate,
            wqb_in, wkb_in, wvb_in, wob_in)
        emit_ffn(S, nc, E, t_const, G, x_in=x3d, x_out=yout, gw=gw_in,
                 w13=w13_in, w2=w2_in, sw13=sw13_in, sw2=sw2_in,
                 t_start=t_attn[-4:])

        with nc.Block() as block:
            S.emit(nc, block)
        S.close()
    return nc


def emit_attn_b(S, nc, E, t_const, G, xseq, xown, x_next,
                lnseq, lnown, kfm, kftok, vtok, qfm_d, qff_d, comb_d,
                cosk, sink, cosq, sinq, gs, gm, t_gate,
                wq, wk, wv, wo):
    from contextlib import ExitStack
    lnseq3 = lnseq.rearrange("(c p) m -> p c m", c=16)
    lnown3 = lnown.rearrange("(c p) m -> p c m", c=16)
    kfm3 = kfm.rearrange("(c p) m -> p c m", c=16)
    qfm3 = qfm_d.rearrange("(c p) m -> p c m", c=16)
    qff3 = qff_d.rearrange("(c p) m -> p c m", c=16)

    with ExitStack() as S0:
        B = S0.enter_context
        NB = 2
        ln_scr = [dict(xt=B(nc.sbuf_tensor(f"b_xt{i}", [128, D], F32)),
                       sq=B(nc.sbuf_tensor(f"b_sq{i}", [128, D], F32)),
                       st=B(nc.sbuf_tensor(f"b_st{i}", [128, 8], F32)),
                       lt=B(nc.sbuf_tensor(f"b_lt{i}", [128, D], BF16)))
                  for i in range(NB)]
        lnfm = [B(nc.sbuf_tensor(f"b_lnfm{i}", [128, 16 * 128], BF16))
                for i in range(NB)]
        prev = {}
        ln_toks = []
        for g in range(NT_SEQ + NT_OWN):
            i = g % NB
            war = prev.get(i, ())
            if g < NT_SEQ:
                src_ap = xseq[g * 128:(g + 1) * 128, :]
                dst_ap = lnseq3[:, :, g * 128:(g + 1) * 128]
            else:
                s = g - NT_SEQ
                src_ap = xown[s * 128:(s + 1) * 128, :]
                dst_ap = lnown3[:, :, s * 128:(s + 1) * 128]
            t_tr, _ = emit_ln(S, src_ap, fm3(lnfm[i], 16), ln_scr[i],
                              deps=t_const + war)
            tst = S.dma("sync", dst_ap, fm3(lnfm[i], 16), deps=(t_tr,))
            prev[i] = (tst,)
            ln_toks.append(tst)
        t_ln = (ln_toks[-1], ln_toks[-2])

    with ExitStack() as S1:
        B = S1.enter_context
        wk_bf = B(nc.sbuf_tensor("b_wk", [128, 16 * 2048], BF16))
        t_wk = S.dma("gpsimd", fm3(wk_bf),
                     wk.rearrange("(c p) m -> p c m", c=16), deps=t_ln)
        NB = 2
        lnt = [B(nc.sbuf_tensor(f"b_lk{i}", [128, 16 * 128], BF16))
               for i in range(NB)]
        cost = [B(nc.sbuf_tensor(f"b_ck{i}", [128, 8 * 128], F32))
                for i in range(NB)]
        sint = [B(nc.sbuf_tensor(f"b_sk{i}", [128, 8 * 128], F32))
                for i in range(NB)]
        krot = [B(nc.sbuf_tensor(f"b_kr{i}", [128, 16 * 128], F32))
                for i in range(NB)]
        kbf = [B(nc.sbuf_tensor(f"b_kb{i}", [128, 16 * 128], BF16))
               for i in range(NB)]
        kffm = [B(nc.sbuf_tensor(f"b_kf{i}", [128, 16 * 128], BF16))
                for i in range(NB)]
        kmin = [B(nc.sbuf_tensor(f"b_km{i}", [128, 16 * 128], F32))
                for i in range(NB)]
        kex = [B(nc.sbuf_tensor(f"b_ke{i}", [128, 16 * 128], F32))
               for i in range(NB)]
        kftr = [B(nc.sbuf_tensor(f"b_kt{i}", [128, 16 * 128], BF16))
                for i in range(NB)]
        p_kA = B(nc.psum_tensor("b_pkA", [128, 128], F32))
        p_kB = B(nc.psum_tensor("b_pkB", [128, 128], F32))
        prev = {}
        k_toks = []
        for g in range(NT_SEQ):
            i = g % NB
            war = prev.get(i, ())
            tl = S.dma("sync", fm3(lnt[i], 16),
                       lnseq3[:, :, g * 128:(g + 1) * 128], deps=war + t_ln)
            tcl = S.dma("sync", fm3(cost[i], 8),
                        cosk.rearrange("(c p) m -> p c m", c=8)[
                            :, :, g * 128:(g + 1) * 128], deps=war)
            tsl = S.dma("sync", fm3(sint[i], 8),
                        sink.rearrange("(c p) m -> p c m", c=8)[
                            :, :, g * 128:(g + 1) * 128], deps=war)
            rope_toks = []
            tpA_prev = prev.get("pk", ())
            for c in range(8):
                tA = S.mmgrp(p_kA[:],
                             [wk_bf[:, k * 2048 + c * 128:
                                    k * 2048 + (c + 1) * 128]
                              for k in range(16)],
                             [lnt[i][:, k * 128:(k + 1) * 128]
                              for k in range(16)],
                             deps=(tl, t_wk) + tpA_prev)
                tB = S.mmgrp(p_kB[:],
                             [wk_bf[:, k * 2048 + (c + 8) * 128:
                                    k * 2048 + (c + 9) * 128]
                              for k in range(16)],
                             [lnt[i][:, k * 128:(k + 1) * 128]
                              for k in range(16)],
                             deps=(tl, t_wk) + tpA_prev)
                csl = cost[i][:, c * 128:(c + 1) * 128]
                ssl = sint[i][:, c * 128:(c + 1) * 128]
                t1 = S.add("vector", lambda e, i=i, c=c, csl=csl: e.tensor_mul(
                    krot[i][:, c * 128:(c + 1) * 128], p_kA[:], csl),
                    deps=(tA, tcl) + war)
                t2 = S.add("vector", lambda e, i=i, c=c, ssl=ssl:
                           e.tensor_mul(kmin[i][:, 0:128], p_kB[:], ssl),
                           deps=(tB, tsl) + war)
                t3 = S.add("vector", lambda e, i=i, c=c: e.tensor_sub(
                    krot[i][:, c * 128:(c + 1) * 128],
                    krot[i][:, c * 128:(c + 1) * 128], kmin[i][:, 0:128]),
                    deps=(t1, t2))
                t4 = S.add("vector", lambda e, i=i, c=c, csl=csl: e.tensor_mul(
                    krot[i][:, (c + 8) * 128:(c + 9) * 128], p_kB[:], csl),
                    deps=(tB, tcl) + war)
                t5 = S.add("vector", lambda e, i=i, c=c, ssl=ssl:
                           e.tensor_mul(kmin[i][:, 128:256], p_kA[:], ssl),
                           deps=(tA, tsl))
                t6 = S.add("vector", lambda e, i=i, c=c: e.tensor_add(
                    krot[i][:, (c + 8) * 128:(c + 9) * 128],
                    krot[i][:, (c + 8) * 128:(c + 9) * 128],
                    kmin[i][:, 128:256]), deps=(t4, t5))
                tpA_prev = (t3, t5, t6)
                rope_toks += [t3, t6]
            prev["pk"] = tpA_prev
            rope_toks = tuple(rope_toks)
            tkb = S.add("vector", lambda e, i=i: e.tensor_copy(
                kbf[i][:], krot[i][:]), deps=rope_toks + war)
            tm1 = S.add("vector", lambda e, i=i: e.tensor_scalar(
                kmin[i][:], krot[i][:], 0.0, None, ALU.min),
                deps=rope_toks + war)
            tm2 = S.add("scalar", lambda e, i=i: e.activation(
                kex[i][:], kmin[i][:], AF.Exp), deps=(tm1,) + war)
            tm3 = S.add("vector", lambda e, i=i: e.tensor_scalar(
                kmin[i][:], krot[i][:], 0.0, None, ALU.max), deps=(tm2,))
            tm4 = S.add("vector", lambda e, i=i: e.tensor_add(
                kffm[i][:], kex[i][:], kmin[i][:]), deps=(tm3,))
            tks = S.dma("sync", kfm3[:, :, g * 128:(g + 1) * 128],
                        fm3(kbf[i], 16), deps=(tkb,))
            ttr = S.trans(fm3(kftr[i], 16), kffm[i][:], deps=(tm4,) + war)
            tkf = S.dma("sync", kftok[g * 128:(g + 1) * 128, :].rearrange(
                "p (c m) -> p c m", c=16), fm3(kftr[i], 16), deps=(ttr,))
            prev[i] = (tks, tkf)
            k_toks += [tks, tkf]
        t_k = tuple(k_toks[-2:])

    with ExitStack() as S2:
        B = S2.enter_context
        wv_bf = B(nc.sbuf_tensor("b_wv", [128, 16 * 2048], BF16))
        t_wv = S.dma("gpsimd", fm3(wv_bf),
                     wv.rearrange("(c p) m -> p c m", c=16), deps=t_k)
        NB = 2
        lnt = [B(nc.sbuf_tensor(f"b_lv{i}", [128, 16 * 128], BF16))
               for i in range(NB)]
        vt = [B(nc.sbuf_tensor(f"b_vt{i}", [128, D], BF16)) for i in range(NB)]
        p_v = [B(nc.psum_tensor(f"b_pv{i}", [128, 512], F32)) for i in range(2)]
        prev = {}
        v_toks = []
        for g in range(NT_SEQ):
            i = g % NB
            war = prev.get(i, ())
            tl = S.dma("sync", fm3(lnt[i], 16),
                       lnseq3[:, :, g * 128:(g + 1) * 128], deps=war + t_ln)
            vts = []
            for oc in range(4):
                ip = oc % 2
                tmm = S.mmgrp(p_v[ip][:],
                              [lnt[i][:, k * 128:(k + 1) * 128]
                               for k in range(16)],
                              [wv_bf[:, k * 2048 + oc * 512:
                                     k * 2048 + oc * 512 + 512]
                               for k in range(16)],
                              deps=(tl, t_wv) + prev.get(("pv", ip), ()))
                tc = S.add("vector", lambda e, i=i, oc=oc, ip=ip:
                           e.tensor_copy(vt[i][:, oc * 512:(oc + 1) * 512],
                                         p_v[ip][:]), deps=(tmm,) + war)
                prev[("pv", ip)] = (tc,)
                vts.append(tc)
            tvs = S.dma("sync", vtok[g * 128:(g + 1) * 128, :], vt[i][:],
                        deps=tuple(vts))
            prev[i] = (tvs,)
            v_toks.append(tvs)
        t_v = tuple(v_toks[-2:])

    with ExitStack() as S3:
        B = S3.enter_context
        wq_bf = B(nc.sbuf_tensor("b_wq", [128, 16 * 2048], BF16))
        t_wq = S.dma("gpsimd", fm3(wq_bf),
                     wq.rearrange("(c p) m -> p c m", c=16), deps=t_v)
        NB = 2
        lnt = [B(nc.sbuf_tensor(f"b_lq{i}", [128, 16 * 128], BF16))
               for i in range(NB)]
        cost = [B(nc.sbuf_tensor(f"b_cq{i}", [128, 8 * 128], F32))
                for i in range(NB)]
        sint = [B(nc.sbuf_tensor(f"b_sq2{i}", [128, 8 * 128], F32))
                for i in range(NB)]
        krot = [B(nc.sbuf_tensor(f"b_qr{i}", [128, 16 * 128], F32))
                for i in range(NB)]
        kbf = [B(nc.sbuf_tensor(f"b_qb{i}", [128, 16 * 128], BF16))
               for i in range(NB)]
        kffm = [B(nc.sbuf_tensor(f"b_qf{i}", [128, 16 * 128], BF16))
                for i in range(NB)]
        kmin = [B(nc.sbuf_tensor(f"b_qm{i}", [128, 16 * 128], F32))
                for i in range(NB)]
        kex = [B(nc.sbuf_tensor(f"b_qe{i}", [128, 16 * 128], F32))
               for i in range(NB)]
        p_kA = B(nc.psum_tensor("b_pqA", [128, 128], F32))
        p_kB = B(nc.psum_tensor("b_pqB", [128, 128], F32))
        prev = {}
        q_toks = []
        for s in range(NT_OWN):
            i = s % NB
            war = prev.get(i, ())
            tl = S.dma("sync", fm3(lnt[i], 16),
                       lnown3[:, :, s * 128:(s + 1) * 128], deps=war + t_ln)
            tcl = S.dma("sync", fm3(cost[i], 8),
                        cosq.rearrange("(c p) m -> p c m", c=8)[
                            :, :, s * 128:(s + 1) * 128], deps=war)
            tsl = S.dma("sync", fm3(sint[i], 8),
                        sinq.rearrange("(c p) m -> p c m", c=8)[
                            :, :, s * 128:(s + 1) * 128], deps=war)
            rope_toks = []
            tpA_prev = prev.get("pq", ())
            for c in range(8):
                tA = S.mmgrp(p_kA[:],
                             [wq_bf[:, k * 2048 + c * 128:
                                    k * 2048 + (c + 1) * 128]
                              for k in range(16)],
                             [lnt[i][:, k * 128:(k + 1) * 128]
                              for k in range(16)],
                             deps=(tl, t_wq) + tpA_prev)
                tB = S.mmgrp(p_kB[:],
                             [wq_bf[:, k * 2048 + (c + 8) * 128:
                                    k * 2048 + (c + 9) * 128]
                              for k in range(16)],
                             [lnt[i][:, k * 128:(k + 1) * 128]
                              for k in range(16)],
                             deps=(tl, t_wq) + tpA_prev)
                csl = cost[i][:, c * 128:(c + 1) * 128]
                ssl = sint[i][:, c * 128:(c + 1) * 128]
                t1 = S.add("vector", lambda e, i=i, c=c, csl=csl: e.tensor_mul(
                    krot[i][:, c * 128:(c + 1) * 128], p_kA[:], csl),
                    deps=(tA, tcl) + war)
                t2 = S.add("vector", lambda e, i=i, ssl=ssl:
                           e.tensor_mul(kmin[i][:, 0:128], p_kB[:], ssl),
                           deps=(tB, tsl) + war)
                t3 = S.add("vector", lambda e, i=i, c=c: e.tensor_sub(
                    krot[i][:, c * 128:(c + 1) * 128],
                    krot[i][:, c * 128:(c + 1) * 128], kmin[i][:, 0:128]),
                    deps=(t1, t2))
                t4 = S.add("vector", lambda e, i=i, c=c, csl=csl: e.tensor_mul(
                    krot[i][:, (c + 8) * 128:(c + 9) * 128], p_kB[:], csl),
                    deps=(tB, tcl) + war)
                t5 = S.add("vector", lambda e, i=i, ssl=ssl:
                           e.tensor_mul(kmin[i][:, 128:256], p_kA[:], ssl),
                           deps=(tA, tsl))
                t6 = S.add("vector", lambda e, i=i, c=c: e.tensor_add(
                    krot[i][:, (c + 8) * 128:(c + 9) * 128],
                    krot[i][:, (c + 8) * 128:(c + 9) * 128],
                    kmin[i][:, 128:256]), deps=(t4, t5))
                tpA_prev = (t3, t5, t6)
                rope_toks += [t3, t6]
            prev["pq"] = tpA_prev
            rope_toks = tuple(rope_toks)
            tkb = S.add("vector", lambda e, i=i: e.tensor_copy(
                kbf[i][:], krot[i][:]), deps=rope_toks + war)
            tm1 = S.add("vector", lambda e, i=i: e.tensor_scalar(
                kmin[i][:], krot[i][:], 0.0, None, ALU.min),
                deps=rope_toks + war)
            tm2 = S.add("scalar", lambda e, i=i: e.activation(
                kex[i][:], kmin[i][:], AF.Exp), deps=(tm1,) + war)
            tm3 = S.add("vector", lambda e, i=i: e.tensor_scalar(
                kmin[i][:], krot[i][:], 0.0, None, ALU.max), deps=(tm2,))
            tm4 = S.add("vector", lambda e, i=i: e.tensor_add(
                kffm[i][:], kex[i][:], kmin[i][:]), deps=(tm3,))
            tqs = S.dma("sync", qfm3[:, :, s * 128:(s + 1) * 128],
                        fm3(kbf[i], 16), deps=(tkb,))
            tqf = S.dma("sync", qff3[:, :, s * 128:(s + 1) * 128],
                        fm3(kffm[i], 16), deps=(tm4,))
            prev[i] = (tqs, tqf)
            q_toks += [tqs, tqf]
        t_q = tuple(q_toks[-2:])

    mem_bf = E(nc.sbuf_tensor("b_mem", [128, 16 * 128], BF16))
    nrm_bf = E(nc.sbuf_tensor("b_nrm", [128, 16], BF16))
    with ExitStack() as S4:
        B = S4.enter_context
        kft = [B(nc.sbuf_tensor(f"b_kft{i}", [128, 8 * 128], BF16)) for i in range(2)]
        vtt = [B(nc.sbuf_tensor(f"b_vtt{i}", [128, 8 * 128], BF16)) for i in range(2)]
        p_m = B(nc.psum_tensor("b_pm", [128, 128], F32))
        p_n = B(nc.psum_tensor("b_pn", [128, 1], F32))
        onesb = G["onesb"]
        prev = {}
        mem_toks = []
        kftok4 = kftok.rearrange("(b p) m -> p b m", b=NT_SEQ)
        vtok4 = vtok.rearrange("(b p) m -> p b m", b=NT_SEQ)
        for h in range(NH):
            tms = []
            tns = []
            for gb in range(NT_SEQ // 8):
                i = gb % 2
                war = prev.get(i, ())
                tk = S.dma("sync", fm3(kft[i], 8),
                           kftok4[:, gb * 8:(gb + 1) * 8,
                                  h * 128:(h + 1) * 128],
                           deps=war + t_k + t_q)
                tv = S.dma("sync", fm3(vtt[i], 8),
                           vtok4[:, gb * 8:(gb + 1) * 8,
                                 h * 128:(h + 1) * 128], deps=war + t_v)
                for gg in range(8):
                    g = gb * 8 + gg
                    tm = S.add("tensor",
                               lambda e, i=i, gg=gg, g=g, NT=NT_SEQ: e.matmul(
                                   p_m[:], kft[i][:, gg * 128:(gg + 1) * 128],
                                   vtt[i][:, gg * 128:(gg + 1) * 128],
                                   start=(g == 0), stop=(g == NT - 1)),
                               deps=(tk, tv) +
                               (prev.get("pm", ()) if g == 0 else ()))
                    tn = S.add("tensor",
                               lambda e, i=i, gg=gg, g=g, NT=NT_SEQ: e.matmul(
                                   p_n[:], kft[i][:, gg * 128:(gg + 1) * 128],
                                   onesb[:], start=(g == 0),
                                   stop=(g == NT - 1)), deps=(tk,) + t_const +
                               (prev.get("pn", ()) if g == 0 else ()))
                    tms.append(tm)
                    tns.append(tn)
                prev[i] = (tms[-1], tns[-1])
            te1 = S.add("vector", lambda e, h=h: e.tensor_copy(
                mem_bf[:, h * 128:(h + 1) * 128], p_m[:]), deps=(tms[-1],))
            te2 = S.add("vector", lambda e, h=h: e.tensor_copy(
                nrm_bf[:, h:h + 1], p_n[:]), deps=(tns[-1],))
            prev["pm"] = (te1,)
            prev["pn"] = (te2,)
            mem_toks += [te1, te2]
        t_mem = tuple(mem_toks)

    comb3 = comb_d.rearrange("(c p) m -> p c m", c=16)
    with ExitStack() as S5:
        B = S5.enter_context
        vh = [B(nc.sbuf_tensor(f"b_vh{i}", [128, 64 * 128], BF16))
              for i in range(2)]
        kt = [B(nc.sbuf_tensor(f"b_ktl{i}", [128, 8 * 128], BF16))
              for i in range(2)]
        pt = [B(nc.sbuf_tensor(f"b_pt{i}", [128, 512], BF16))
              for i in range(2)]
        qsl = [B(nc.sbuf_tensor(f"b_qsl{i}", [128, 512], BF16))
               for i in range(2)]
        qfsl = [B(nc.sbuf_tensor(f"b_qfsl{i}", [128, 512], BF16))
                for i in range(2)]
        rden = B(nc.sbuf_tensor("b_rden", [1, 512], F32))
        rlin = B(nc.sbuf_tensor("b_rlin", [1, 512], F32))
        rdsb = B(nc.sbuf_tensor("b_rdsb", [128, 512], F32))
        rlsb = B(nc.sbuf_tensor("b_rlsb", [128, 512], F32))
        ct1 = B(nc.sbuf_tensor("b_ct1", [128, 512], F32))
        combt = B(nc.sbuf_tensor("b_combt", [128, 512], BF16))
        p_s = [B(nc.psum_tensor(f"b_ps{i}", [128, 512], F32)) for i in range(2)]
        p_oo = [B(nc.psum_tensor(f"b_po{i}", [128, 512], F32)) for i in range(2)]
        p_css = [B(nc.psum_tensor(f"b_pcs{i}", [1, 512], F32)) for i in range(2)]
        p_m2 = B(nc.psum_tensor("b_pm2", [128, 512], F32))
        p_misc = B(nc.psum_tensor("b_pmisc", [128, 512], F32))
        onesf, onesb = G["onesf"], G["onesb"]
        vtok3 = vtok.rearrange("(g p) m -> p g m", g=64)
        last = {}
        t_comb = []
        for p in range(4):
            for h in range(NH):
                iv = h % 2
                iq = (p * NH + h) % 2
                io = h % 2
                p_o = p_oo[io]
                p_cs = p_css[io]
                tvl = S.dma("sync", fm3(vh[iv], 64),
                            vtok3[:, :, h * 128:(h + 1) * 128],
                            deps=t_v + t_mem + last.get(("v", iv), ()))
                tql = S.dma("sync", qsl[iq][:],
                            qfm_d[h * 128:(h + 1) * 128,
                                  p * 512:(p + 1) * 512],
                            deps=t_q + last.get(("q", iq), ()))
                tqfl = S.dma("sync", qfsl[iq][:],
                             qff_d[h * 128:(h + 1) * 128,
                                   p * 512:(p + 1) * 512],
                             deps=t_q + last.get(("q", iq), ()))
                t_cs_last = None
                t_pv_last = None
                tkls = {}
                for t in range(NT_SEQ):
                    ip = t % 2
                    if t % 8 == 0:
                        ik = (t // 8) % 2
                        tkls[ik] = S.dma(
                            "sync", kt[ik][:],
                            kfm[h * 128:(h + 1) * 128,
                                t * 128:(t + 8) * 128],
                            deps=t_k + last.get(("k", ik), ()))
                    ik = (t // 8) % 2
                    tkl = tkls[ik]
                    ksl = kt[ik][:, (t % 8) * 128:(t % 8 + 1) * 128]
                    tmm = S.add("tensor", lambda e, ksl=ksl, ip=ip, iq=iq:
                                e.matmul(p_s[ip][:], ksl, qsl[iq][:],
                                         start=True, stop=True),
                                deps=(tkl, tql) + last.get(("s", ip), ()))
                    if t % 8 == 7:
                        last[("k", ik)] = (tmm,)
                    texp = S.add("scalar", lambda e, ip=ip: e.activation(
                        pt[ip][:], p_s[ip][:], AF.Exp, scale=INV_SQRT_DH),
                        deps=(tmm,) + last.get(("pt", ip), ()))
                    last[("s", ip)] = (texp,)
                    tcs = S.add("tensor", lambda e, ip=ip, t=t, p_cs=p_cs:
                                e.matmul(p_cs[:], onesb[:], pt[ip][:],
                                         start=(t == 0),
                                         stop=(t == NT_SEQ - 1)),
                                deps=(texp,) + t_const +
                                (last.get(("pcs", io), ()) if t == 0 else ()))
                    tpv = S.add("tensor", lambda e, ip=ip, t=t, iv=iv, p_o=p_o:
                                e.matmul(p_o[:],
                                         vh[iv][:, t * 128:(t + 1) * 128],
                                         pt[ip][:], start=(t == 0),
                                         stop=(t == NT_SEQ - 1)),
                                deps=(texp, tvl) +
                                (last.get(("po", io), ()) if t == 0 else ()))
                    last[("pt", ip)] = (tcs, tpv)
                    t_cs_last, t_pv_last = tcs, tpv
                last[("v", iv)] = (t_pv_last,)
                tlm = S.add("tensor", lambda e, h=h, iq=iq: e.matmul(
                    p_m2[:], mem_bf[:, h * 128:(h + 1) * 128], qfsl[iq][:],
                    start=True, stop=True),
                    deps=(tqfl,) + t_mem + last.get("pm2", ()))
                tln = S.add("tensor", lambda e, h=h, iq=iq: e.matmul(
                    p_misc[0:1, :], nrm_bf[:, h:h + 1], qfsl[iq][:],
                    start=True, stop=True),
                    deps=(tqfl,) + t_mem + last.get("pdn", ()))
                last[("q", iq)] = (tmm, tlm, tln)
                trd = S.add("vector", lambda e, p_cs=p_cs: e.reciprocal(
                    rden[:], p_cs[:]), deps=(t_cs_last,) + last.get("rdw", ()))
                last[("pcs", io)] = (trd,)
                trl = S.add("vector", lambda e: e.reciprocal(
                    rlin[:], p_misc[0:1, :]), deps=(tln,) + last.get("rlw", ()))
                tgd = S.add("vector", lambda e, h=h: e.tensor_scalar(
                    rden[:], rden[:], gm[0:1, h:h + 1], None, ALU.mult),
                    deps=(trd,) + t_gate)
                tgl = S.add("vector", lambda e, h=h: e.tensor_scalar(
                    rlin[:], rlin[:], gs[0:1, h:h + 1], None, ALU.mult),
                    deps=(trl,) + t_gate)
                last["rdw"] = (tgd,)
                last["rlw"] = (tgl,)
                tb1 = S.add("tensor", lambda e: e.matmul(
                    p_misc[:], onesf[:], rden[:], start=True, stop=True),
                    deps=(tgd, trl) + last.get("bc", ()))
                tc1 = S.add("vector", lambda e: e.tensor_copy(
                    rdsb[:], p_misc[:]), deps=(tb1,) + last.get("rdsb", ()))
                tb2 = S.add("tensor", lambda e: e.matmul(
                    p_misc[:], onesf[:], rlin[:], start=True, stop=True),
                    deps=(tgl, tc1))
                tc2 = S.add("vector", lambda e: e.tensor_copy(
                    rlsb[:], p_misc[:]), deps=(tb2,) + last.get("rlsb", ()))
                last["bc"] = (tc2,)
                last["pdn"] = (tc2,)
                tx1 = S.add("vector", lambda e, p_o=p_o: e.tensor_mul(
                    ct1[:], p_o[:], rdsb[:]),
                    deps=(tc1, t_pv_last) + last.get("ct1", ()))
                last["rdsb"] = (tx1,)
                last[("po", io)] = (tx1,)
                tx3 = S.add("vector", lambda e: e.tensor_mul(
                    rlsb[:], p_m2[:], rlsb[:]), deps=(tc2, tlm))
                last["rlsb"] = (tx3,)
                last["pm2"] = (tx3,)
                tx4 = S.add("vector", lambda e: e.tensor_add(
                    combt[:], ct1[:], rlsb[:]),
                    deps=(tx3, tx1) + last.get("combt", ()))
                last["ct1"] = (tx4,)
                tcw = S.dma("sync", comb_d[h * 128:(h + 1) * 128,
                                          p * 512:(p + 1) * 512],
                            combt[:], deps=(tx4,))
                last["combt"] = (tcw,)
                t_comb.append(tcw)
        t_comb = tuple(t_comb)

    with ExitStack() as S6:
        B = S6.enter_context
        wo_bf = B(nc.sbuf_tensor("b_wobf", [128, 16 * 2048], BF16))
        t_wo = S.dma("gpsimd", fm3(wo_bf),
                     wo.rearrange("(c p) m -> p c m", c=16), deps=t_comb[-2:])
        cmb = [B(nc.sbuf_tensor(f"b_cmb{i}", [128, 16 * 128], BF16))
               for i in range(2)]
        xo = [B(nc.sbuf_tensor(f"b_xo{i}", [128, 512], F32)) for i in range(2)]
        x1t = [B(nc.sbuf_tensor(f"b_x1t{i}", [128, 512], F32))
               for i in range(2)]
        p_y = [B(nc.psum_tensor(f"b_py{i}", [128, 512], F32))
               for i in range(2)]
        pr = {}
        t_res = []
        for s in range(NT_OWN):
            i = s % 2
            tcl = S.dma("sync", fm3(cmb[i], 16),
                        comb3[:, :, s * 128:(s + 1) * 128],
                        deps=t_comb[-2:] + pr.get(("c", i), ()))
            for oc in range(4):
                ii = (s * 4 + oc) % 2
                war = pr.get(ii, ())
                tmm = S.mmgrp(
                    p_y[ii][:],
                    [cmb[i][:, c * 128:(c + 1) * 128] for c in range(16)],
                    [wo_bf[:, c * 2048 + oc * 512:c * 2048 + oc * 512 + 512]
                     for c in range(16)],
                    deps=(tcl, t_wo) + war)
                txl = S.dma("sync", xo[ii][:],
                            xown[s * 128:(s + 1) * 128,
                                 oc * 512:(oc + 1) * 512], deps=war)
                tad = S.add("vector", lambda e, ii=ii: e.tensor_add(
                    x1t[ii][:], p_y[ii][:], xo[ii][:]), deps=(tmm, txl))
                tst = S.dma("sync",
                            x_next[s * 128:(s + 1) * 128,
                                   oc * 512:(oc + 1) * 512],
                            x1t[ii][:], deps=(tad,))
                pr[ii] = (tst, tad)
                pr[("c", i)] = pr.get(("c", i), ()) + (tmm,)
                t_res.append(tst)
        return tuple(t_res)


def _own_rows(j):
    idx = []
    for k in range(NT_OWN):
        g = 4 * k + j
        idx.append(np.arange(g * 128, (g + 1) * 128))
    return np.concatenate(idx)


def _prep_common(inputs):
    P = {}
    P["iota"] = np.arange(128, dtype=np.float32).reshape(128, 1)
    P["onesf"] = np.ones((1, 128), np.float32)
    P["onesb"] = np.ones((128, 1), BF)
    P["ident"] = np.eye(128, dtype=np.float32)
    P["cst"] = np.zeros((128, 8), np.float32)
    P["cst"][:, 0] = EPS
    P["cst"][:, 1] = INV_SQRT_DH
    P["gwT"] = np.ascontiguousarray(np.asarray(inputs["L0_ffn_gate_w"]).T)
    w13T = np.asarray(inputs["L0_ffn_w13"]).transpose(0, 2, 1)
    w13r = w13T.reshape(NE, 16, 128, 22, 128)
    P["w13L"] = np.ascontiguousarray(w13r.transpose(0, 3, 2, 1, 4).reshape(
        NE, 22, 128, 16 * 128))
    w2T = np.asarray(inputs["L0_ffn_w2"]).transpose(0, 2, 1)
    w2r = w2T.reshape(NE, 11, 128, 2, 1024)
    P["w2L"] = np.ascontiguousarray(w2r.transpose(0, 3, 2, 1, 4).reshape(
        NE, 2, 128, 11 * 1024))
    sw13T = np.asarray(inputs["L0_ffn_sw13"]).T
    sw13r = sw13T.reshape(16, 128, 44, 128)
    P["sw13L"] = np.ascontiguousarray(sw13r.transpose(2, 1, 0, 3).reshape(
        44, 128, 16 * 128))
    sw2T = np.asarray(inputs["L0_ffn_sw2"]).T
    sw2r = sw2T.reshape(22, 128, 2, 1024)
    P["sw2L"] = np.ascontiguousarray(sw2r.transpose(2, 1, 0, 3).reshape(
        2, 128, 22 * 1024))
    return P


def _prep_l1(inputs):
    P = _prep_common(inputs)
    WqT = np.asarray(inputs["L0_attn_a_W_q"]).T
    WkvT = np.asarray(inputs["L0_attn_a_W_kv"]).T
    qperm = [r * 144 + 16 + d for r in range(6) for d in range(128)] + \
            [r * 144 + h for r in range(6) for h in range(16)]
    kvperm = [r * 144 + 16 + d for r in range(4) for d in range(128)] + \
             [r * 144 + h for r in range(4) for h in range(16)]
    P["wqT"] = np.ascontiguousarray(WqT[:, qperm])
    P["wkvT"] = np.ascontiguousarray(WkvT[:, kvperm])
    P["woT"] = np.ascontiguousarray(np.asarray(inputs["L0_attn_a_W_o"]).T)
    inv = (1.0 / 10000.0) ** (np.arange(0, DHD, 2, dtype=np.float64) / DHD)
    t = np.arange(SEQ, dtype=np.float64)
    fr = t[:, None] * inv[None, :]
    P["cosa"] = np.cos(fr).astype(np.float32)
    P["sina"] = np.sin(fr).astype(np.float32)
    return P


def _prep_l2(inputs):
    P = _prep_common(inputs)
    P["wqbT"] = np.ascontiguousarray(np.asarray(inputs["L0_attn_b_Wq"]).T)
    P["wkbT"] = np.ascontiguousarray(np.asarray(inputs["L0_attn_b_Wk"]).T)
    P["wvbT"] = np.ascontiguousarray(np.asarray(inputs["L0_attn_b_Wv"]).T)
    P["wobT"] = np.ascontiguousarray(np.asarray(inputs["L0_attn_b_Wo"]).T)
    P["gater"] = np.ascontiguousarray(
        np.asarray(inputs["L0_attn_b_gate"]).reshape(1, NH)).astype(np.float32)
    inv = 1.0 / 10000.0 ** (np.arange(0, D, 2, dtype=np.float64) / D)
    t = np.arange(SEQ, dtype=np.float64)
    fr = np.outer(inv, t)
    P["coskf"] = np.cos(fr).astype(np.float32)
    P["sinkf"] = np.sin(fr).astype(np.float32)
    return P


_BUILD_CACHE = {}


def _cached(name, fn):
    if name not in _BUILD_CACHE:
        _BUILD_CACHE[name] = fn()
    return _BUILD_CACHE[name]


def run_launch1(inputs):
    x0 = np.asarray(inputs["L0_attn_a_x"], np.float32)
    P1 = _prep_l1(inputs)
    nc1 = _cached("l1", build_launch1)
    rows = [_own_rows(j) for j in range(4)]
    in_maps = []
    for c in range(NCORE):
        b, j = c // 4, c % 4
        m = dict(P1)
        m["xseq"] = x0[b]
        m["xown"] = np.ascontiguousarray(x0[b][rows[j]])
        m["qpos"] = rows[j].astype(np.float32).reshape(1, TCORE)
        in_maps.append(m)
    res1 = run_bass_kernel_spmd(nc1, in_maps, core_ids=list(range(NCORE)))
    x2 = np.empty_like(x0)
    for c in range(NCORE):
        b, j = c // 4, c % 4
        x2[b][rows[j]] = res1.results[c]["x2o"]
    return x2


def kernel(**inputs):
    x0 = np.asarray(inputs["L0_attn_a_x"], np.float32)
    x2 = run_launch1(inputs)

    P2 = _prep_l2(inputs)
    nc2 = _cached("l2", build_launch2)
    rows = [_own_rows(j) for j in range(4)]
    in_maps2 = []
    for c in range(NCORE):
        b, j = c // 4, c % 4
        m = dict(P2)
        m["xseq"] = x2[b]
        m["xown"] = np.ascontiguousarray(x2[b][rows[j]])
        m["cosqf"] = np.ascontiguousarray(P2["coskf"][:, rows[j]])
        m["sinqf"] = np.ascontiguousarray(P2["sinkf"][:, rows[j]])
        in_maps2.append(m)
    res2 = run_bass_kernel_spmd(nc2, in_maps2, core_ids=list(range(NCORE)))
    y = np.empty_like(x0)
    for c in range(NCORE):
        b, j = c // 4, c % 4
        y[b][rows[j]] = res2.results[c]["y"]
    return y
